# version 07
# baseline (speedup 1.0000x reference)
import contextlib
import numpy as np
import ml_dtypes
import concourse.bass as bass
import concourse.mybir as mybir
from concourse.bass_utils import run_bass_kernel_spmd

F32 = mybir.dt.float32
BF16 = mybir.dt.bfloat16
I32 = mybir.dt.int32
I16 = mybir.dt.int16
Alu = mybir.AluOpType
Act = mybir.ActivationFunctionType
AX = mybir.AxisListType.X

B, S, D = 4, 2048, 2048
H, KVH, DH = 16, 4, 128
NCORES = 8
NEG = -1e30
TQ = 1024

QBLOCKS = [[0, 3, 4, 7, 8, 11, 12, 15], [1, 2, 5, 6, 9, 10, 13, 14]]
MPAT = [1, 3, 5, 7, 9, 11, 13, 15]

ENGS = ("sync", "tensor", "vector", "scalar", "gpsimd")


NSLOT = 8


class Build:

    def __init__(self):
        self.entries = {e: [] for e in ENGS}
        self.cnt = {}
        self.slot_i = {"sync": 0, "gpsimd": 0}

    def emit(self, eng, f, wait=(), inc=()):
        self.entries[eng].append((tuple(wait), f, tuple(inc)))
        out = None
        for s, n in inc:
            self.cnt[s] = self.cnt.get(s, 0) + n
            out = self.cnt[s]
        return out

    def dma(self, eng, f, wait=()):
        q = "ld" if eng == "sync" else "gd"
        i = self.slot_i[eng] % NSLOT
        self.slot_i[eng] += 1
        s = f"{q}{i}"
        uses = self.cnt.get(s, 0)
        w = list(wait)
        if uses:
            w.append((s, uses))
        v = self.emit(eng, f, wait=w, inc=[(s, 16)])
        return (s, v)

    def dma_totals(self, eng):
        q = "ld" if eng == "sync" else "gd"
        return [(f"{q}{i}", self.cnt.get(f"{q}{i}", 0)) for i in range(NSLOT)
                if self.cnt.get(f"{q}{i}", 0)]

    def cur(self, s):
        return self.cnt.get(s, 0)


def build_nc():
    nc = bass.Bass("TRN2", target_bir_lowering=False, debug=False)
    KJ, NB = 1024, 64

    x_full = nc.dram_tensor("x_full", [S, D], F32, kind="ExternalInput").ap()
    x_q = nc.dram_tensor("x_q", [TQ, D], F32, kind="ExternalInput").ap()
    wq_h = nc.dram_tensor("wq", [D, D], F32, kind="ExternalInput").ap()
    wk_h = nc.dram_tensor("wk", [D, KVH * DH], F32, kind="ExternalInput").ap()
    wv_h = nc.dram_tensor("wv", [D, KVH * DH], F32, kind="ExternalInput").ap()
    wo_h = nc.dram_tensor("wo", [D, D], F32, kind="ExternalInput").ap()
    gp_h = nc.dram_tensor("gp", [2048, KJ], I32, kind="ExternalInput").ap()
    gs_h = nc.dram_tensor("gs", [2048, NB], F32, kind="ExternalInput").ap()
    up_h = nc.dram_tensor("up", [2048, KJ], I32, kind="ExternalInput").ap()
    us_h = nc.dram_tensor("us", [2048, NB], F32, kind="ExternalInput").ap()
    dp_h = nc.dram_tensor("dp", [2048, KJ], I32, kind="ExternalInput").ap()
    ds_h = nc.dram_tensor("ds", [2048, NB], F32, kind="ExternalInput").ap()
    qg_h = nc.dram_tensor("qg", [128, 1], F32, kind="ExternalInput").ap()
    kg_h = nc.dram_tensor("kg", [128, 1], F32, kind="ExternalInput").ap()
    id_h = nc.dram_tensor("ident", [128, 128], BF16, kind="ExternalInput").ap()
    idf_h = nc.dram_tensor("identf", [128, 128], F32, kind="ExternalInput").ap()
    mk_h = nc.dram_tensor("maskqb", [8, 128, 512], F32, kind="ExternalInput").ap()
    y_h = nc.dram_tensor("y", [TQ, D], F32, kind="ExternalOutput").ap()

    wqb_h = nc.dram_tensor("wq_bf", [D, D], BF16).ap()
    wkb_h = nc.dram_tensor("wk_bf", [D, KVH * DH], BF16).ap()
    wvb_h = nc.dram_tensor("wv_bf", [D, KVH * DH], BF16).ap()
    wob_h = nc.dram_tensor("wo_bf", [D, D], BF16).ap()
    g_sh = nc.dram_tensor("g_sh", [16, 128, 16, 128], BF16)
    u_sh = nc.dram_tensor("u_sh", [16, 128, 16, 128], BF16)
    d_sh = nc.dram_tensor("d_sh", [16, 128, 16, 128], BF16)
    g_fu = nc.dram_tensor("g_fu", [128, 128, 16, 128], BF16)
    u_fu = nc.dram_tensor("u_fu", [128, 128, 16, 128], BF16)
    d_fu = nc.dram_tensor("d_fu", [128, 128, 16, 128], BF16)

    bld = Build()
    E, C = bld.emit, bld.cur

    def vbar():
        return E("vector", lambda e: e.drain(), inc=[("v", 1)])

    def sbar():
        return E("scalar", lambda e: e.drain(), inc=[("s", 1)])

    with contextlib.ExitStack() as ST:
        alloc = ST.enter_context

        ar1 = alloc(nc.sbuf_tensor("ar1", [128, 32768], BF16))
        ar2 = alloc(nc.sbuf_tensor("ar2", [128, 16384], BF16))
        ar3 = alloc(nc.sbuf_tensor("ar3", [128, 16384], BF16))
        ar4 = alloc(nc.sbuf_tensor("ar4", [128, 8192], BF16))
        ar5 = alloc(nc.sbuf_tensor("ar5", [128, 8192], BF16))
        ar6 = alloc(nc.sbuf_tensor("ar6", [128, 8192], BF16))
        ar7 = alloc(nc.sbuf_tensor("ar7", [128, 8192], BF16))
        arp = alloc(nc.sbuf_tensor("arp", [128, 4096], BF16))
        id_sb = alloc(nc.sbuf_tensor("id_sb", [128, 128], BF16))
        idf_sb = alloc(nc.sbuf_tensor("idf_sb", [128, 128], F32))
        qg_sb = alloc(nc.sbuf_tensor("qg_sb", [128, 1], F32))
        kg_sb = alloc(nc.sbuf_tensor("kg_sb", [128, 1], F32))
        ones_b = alloc(nc.sbuf_tensor("ones_b", [128, 1], BF16))
        ones_f = alloc(nc.sbuf_tensor("ones_f", [1, 128], F32))
        nmax = alloc(nc.sbuf_tensor("nmax", [128, 1], F32))
        rsum = alloc(nc.sbuf_tensor("rsum", [128, 1], F32))
        rcp = alloc(nc.sbuf_tensor("rcp", [128, 1], F32))
        nmean = alloc(nc.sbuf_tensor("nmean", [128, 1], F32))
        ssd = alloc(nc.sbuf_tensor("ssd", [128, 1], F32))
        rstd = alloc(nc.sbuf_tensor("rstd", [128, 1], F32))
        sq2 = alloc(nc.sbuf_tensor("sq2", [128, 2048], BF16))
        eps8 = alloc(nc.sbuf_tensor("eps8", [128, 1], F32))
        eps5 = alloc(nc.sbuf_tensor("eps5", [128, 1], F32))
        ps = alloc(nc.psum_tensor("ps", [128, 4096], F32))

        SEM = {}
        semnames = ["t", "v", "s", "cc", "wc"]
        semnames += [f"ld{i}" for i in range(NSLOT)] + [f"gd{i}" for i in range(NSLOT)]
        for s_ in semnames:
            SEM[s_] = alloc(nc.semaphore(s_))
        blk = alloc(nc.Block())

        def bank(i, n=1):
            return ps[:, 512 * i:512 * (i + n)]

        def bank_bf(i, n=1):
            return ps[:, 512 * i:512 * (i + n)].bitcast(BF16)

        xfT = ar1[:].rearrange("p (a b) -> p a b", b=2048)
        wo_sb = ar1[:].rearrange("p (a b) -> p a b", b=2048)
        y2acc = ar1[:].bitcast(F32).rearrange("p (a b) -> p a b", b=1024)
        xqT = ar2[:].rearrange("p (a b) -> p a b", b=1024)
        wv_sb = ar2[:, 8192:16384].rearrange("p (a b) -> p a b", b=512)
        ao_sb = ar2[:].rearrange("p (a b) -> p a b", b=1024)
        inter = ar2[:].rearrange("p (a b) -> p a b", b=1024)
        qhat = ar3[:].rearrange("p (a b) -> p a b", b=1024)
        y1T = ar3[:].rearrange("p (a b) -> p a b", b=1024)
        khat = ar4[:].rearrange("p (a b) -> p a b", b=2048)
        gub = ar4[:].rearrange("p (a b c) -> p a b c", b=16, c=128)
        v_sb = ar5[:].rearrange("p (a b) -> p a b", b=512)
        dnb = ar5[:, 0:4096].rearrange("p (a b c) -> p a b c", b=16, c=128)
        y1pre = ar5[:, 0:4096].bitcast(F32)
        dq_b16 = [ar2[:, 0:1024].bitcast(I16), ar2[:, 1024:2048].bitcast(I16)]
        dq_m = ar2[:, 2048:3072].bitcast(I16)
        dq_m2 = ar2[:, 3072:4096].bitcast(I16)
        dq_a1 = ar2[:, 4096:5120].bitcast(I16)
        dq_a1h = ar2[:, 5120:6144].bitcast(I16)
        dq_bb = ar2[:, 6144:7168].bitcast(I16)
        dq_bbh = ar2[:, 7168:8192].bitcast(I16)
        dq_s2 = ar2[:, 8192:9216].bitcast(I16)
        dq_s2h = ar2[:, 9216:10240].bitcast(I16)
        dq_bits = ar2[:, 10240:12288].bitcast(I16).rearrange("p (a b) -> p a b", b=1024)
        dq_w = ar2[:, 10240:12288].rearrange("p (a b) -> p a b", b=1024)
        dq_sc = [ar2[:, 12288:12416].bitcast(F32), ar2[:, 12416:12544].bitcast(F32)]
        dq_st = ar2[:, 12544:14592].rearrange("p (a b) -> p a b", b=128)
        masks = ar7[:].bitcast(F32).rearrange("p (a b) -> p a b", b=512)
        xtok = [ar6[:, 0:2048], ar6[:, 2048:4096]]
        wqbuf = [ar6[:, 4096:4352], ar6[:, 4352:4608]]
        kraw = ar6[:, 5632:7680]
        qraw = kraw.rearrange("p (a b) -> p a b", b=1024)
        xres = [ar6[:, 0:4096].bitcast(F32), ar6[:, 4096:8192].bitcast(F32)]
        y2tok = [ar6[:, 0:4096].bitcast(F32), ar6[:, 4096:8192].bitcast(F32)]
        rms_r = ar6[:, 4608:5632].bitcast(F32)[0:1, 0:512]
        ln2a = arp[:, 0:1024].bitcast(F32)[0:1, 0:512]
        ln2b = arp[:, 1024:2048].bitcast(F32)[0:1, 0:512]
        ln2c = arp[:, 2048:3072].bitcast(F32)[0:1, 0:512]
        P_sb = arp[:, 0:2048]
        PT_sb = arp[:, 2048:4096].rearrange("p (a b) -> p a b", b=128)

        DMA = bld.dma
        id_ms = DMA("sync", lambda e: e.dma_start(out=id_sb[:], in_=id_h))
        idf_ms = DMA("sync", lambda e: e.dma_start(out=idf_sb[:], in_=idf_h))
        qg_ms = DMA("sync", lambda e: e.dma_start(out=qg_sb[:], in_=qg_h))
        kg_ms = DMA("sync", lambda e: e.dma_start(out=kg_sb[:], in_=kg_h))
        for qb in range(8):
            DMA("sync", lambda e, qb=qb: e.dma_start(out=masks[:, qb, :], in_=mk_h[qb]))
        const_ms = bld.dma_totals("sync")
        E("vector", lambda e: e.memset(ones_b[:], 1.0))
        E("vector", lambda e: e.memset(ones_f[:], 1.0))
        E("vector", lambda e: e.memset(eps8[:], 1e-8))
        E("vector", lambda e: e.memset(eps5[:], 1e-5))
        E("vector", lambda e: e.drain())
        for src, dst in ((wq_h, wqb_h), (wk_h, wkb_h), (wv_h, wvb_h), (wo_h, wob_h)):
            E("gpsimd", lambda e, s_=src, d_=dst: e.dma_start(out=d_, in_=s_),
              inc=[("wc", 16)])
        wcast_done = C("wc")

        def dq_chain(v, bu):
            b16v = dq_b16[bu]
            v.tensor_single_scalar(dq_m[:], b16v[:], 7, Alu.bitwise_and)
            v.tensor_single_scalar(dq_m2[:], b16v[:], 0x70, Alu.bitwise_and)
            v.tensor_single_scalar(dq_s2[:], b16v[:], 8, Alu.bitwise_and)
            v.tensor_single_scalar(dq_s2h[:], b16v[:], 128, Alu.bitwise_and)
            v.drain()
            v.tensor_single_scalar(dq_a1[:], dq_m[:], 64, Alu.mult)
            v.tensor_scalar(dq_bb[:], dq_m[:], 2, 16128, op0=Alu.min, op1=Alu.mult)
            v.tensor_single_scalar(dq_a1h[:], dq_m2[:], 4, Alu.mult)
            v.tensor_scalar(dq_bbh[:], dq_m2[:], 32, 1008, op0=Alu.min, op1=Alu.mult)
            v.tensor_single_scalar(dq_s2[:], dq_s2[:], -4096, Alu.mult)
            v.tensor_single_scalar(dq_s2h[:], dq_s2h[:], -256, Alu.mult)
            v.drain()
            v.scalar_tensor_tensor(dq_a1[:], dq_a1[:], 16128.0, dq_bb[:],
                                   op0=Alu.add, op1=Alu.min)
            v.scalar_tensor_tensor(dq_a1h[:], dq_a1h[:], 16128.0, dq_bbh[:],
                                   op0=Alu.add, op1=Alu.min)
            v.drain()
            v.tensor_tensor(dq_bits[:, 0, :], dq_a1[:], dq_s2[:], Alu.add)
            v.tensor_tensor(dq_bits[:, 1, :], dq_a1h[:], dq_s2h[:], Alu.add)
            v.drain()
            sc_b = dq_sc[bu][:].unsqueeze(2).broadcast_to([128, 64, 16])
            for pl in range(2):
                bits_bf = dq_bits[:, pl, :].bitcast(BF16).rearrange(
                    "n (b u) -> n b u", u=16)
                v.tensor_tensor(dq_w[:, pl, :].rearrange("n (b u) -> n b u", u=16),
                                bits_bf, sc_b, Alu.mult)
            return v.drain()

        dq_cons = [0, 0]
        tr_done = 0
        evac_ms = [0, 0]
        stage_ms = 0
        for pk, scl, sh in ((gp_h, gs_h, g_sh), (up_h, us_h, u_sh),
                            (dp_h, ds_h, d_sh)):
            for oc in range(16):
                bu = oc % 2
                wld = [("v", dq_cons[bu])] if dq_cons[bu] else []
                m1 = DMA("gpsimd", lambda e, pk=pk, oc=oc, bu=bu: e.dma_start(
                    out=dq_b16[bu][:],
                    in_=pk[oc * 128:oc * 128 + 128, :].rearrange(
                        "(j two) c -> two j c", two=2)), wait=wld)
                m2 = DMA("sync", lambda e, scl=scl, oc=oc, bu=bu: e.dma_start(
                    out=dq_sc[bu][:],
                    in_=scl[oc * 128:oc * 128 + 128, :].rearrange(
                        "(j two) c -> two j c", two=2)), wait=wld)
                wv_ = [m1, m2]
                if tr_done:
                    wv_.append(("t", tr_done))
                E("vector", lambda e, bu=bu: dq_chain(e, bu), wait=wv_)
                dq_cons[bu] = vm = E("vector", lambda e: e.drain(), inc=[("v", 1)])
                for bh in range(2):
                    wt = [("v", vm)]
                    if evac_ms[bh]:
                        wt.append(("v", evac_ms[bh]))
                    for kcl in range(8):
                        kc = bh * 8 + kcl
                        E("tensor", lambda e, bh=bh, kcl=kcl, kc=kc: e.transpose(
                            bank_bf(6 + bh)[0:64, kcl * 128:kcl * 128 + 128],
                            dq_w[:, 0, kc * 64:kc * 64 + 64], id_sb[:]),
                          wait=wt if kcl == 0 else ())
                        E("tensor", lambda e, bh=bh, kcl=kcl, kc=kc: e.transpose(
                            bank_bf(6 + bh)[64:128, kcl * 128:kcl * 128 + 128],
                            dq_w[:, 1, kc * 64:kc * 64 + 64], id_sb[:]),
                          inc=[("t", 1)] if kcl == 7 else ())
                    tm = C("t")
                    wev = [("t", tm)]
                    if stage_ms:
                        wev.append(stage_ms)
                    evac_ms[bh] = E("vector", lambda e, bh=bh: e.tensor_copy(
                        dq_st[:, bh * 8:bh * 8 + 8, :],
                        bank_bf(6 + bh)[:].rearrange("p (a b) -> p a b", b=128)),
                        wait=wev, inc=[("v", 1)])
                tr_done = tm
                stage_ms = DMA("gpsimd", lambda e, sh=sh, oc=oc: e.dma_start(
                    out=sh.ap()[oc], in_=dq_st[:]), wait=[("v", evac_ms[1])])
        ag_w = bld.dma_totals("gpsimd")
        for sh_, fu_ in ((g_sh, g_fu), (u_sh, u_fu), (d_sh, d_fu)):
            E("gpsimd", lambda e, sh_=sh_, fu_=fu_: e.collective_compute(
                "AllGather", Alu.bypass, replica_groups=[list(range(NCORES))],
                ins=[sh_.ap().opt()], outs=[fu_.ap().opt()]),
              wait=ag_w, inc=[("cc", 1)])
            ag_w = ()
        ag_done = C("cc")
        dq_all_done = stage_ms

        xa_ms = [0, 0]
        ev45 = [0, 0]
        for which, (ntt, dst) in enumerate(((16, xfT), (8, xqT))):
            xd = x_full if which == 0 else x_q
            for tt in range(ntt):
                bu = tt % 2
                wt0 = [("t", xa_ms[bu])] if xa_ms[bu] else []
                gm = DMA("gpsimd", lambda e, xd=xd, tt=tt, bu=bu: e.dma_start(
                    out=xtok[bu][:], in_=xd[tt * 128:tt * 128 + 128, :]),
                    wait=wt0)
                for bh in range(2):
                    wt2 = [gm]
                    if ev45[bh]:
                        wt2.append(("s", ev45[bh]))
                    for kcl in range(8):
                        kc = bh * 8 + kcl
                        E("tensor", lambda e, bu=bu, bh=bh, kcl=kcl, kc=kc: e.transpose(
                            bank_bf(4 + bh)[:, kcl * 128:kcl * 128 + 128],
                            xtok[bu][:, kc * 128:kc * 128 + 128], id_sb[:]),
                          wait=wt2 if kcl == 0 else (),
                          inc=[("t", 1)] if kcl == 7 else ())
                    tm = C("t")
                    if bh == 1:
                        xa_ms[bu] = tm
                    wv2 = [("t", tm)]
                    if which == 1 and tt == 0 and bh == 0:
                        wv2.append(dq_all_done)
                    ev45[bh] = E("scalar", lambda e, dst=dst, bh=bh, tt=tt: e.copy(
                        dst[:, bh * 8:bh * 8 + 8, tt * 128:tt * 128 + 128],
                        bank_bf(4 + bh)[:].rearrange("p (a b) -> p a b", b=128)),
                        wait=wv2, inc=[("s", 1)])
        xT_done_v = ("s", C("s"))

        def rms_apply(raw_ap, out_ap, nblk, reg, gam):
            sqv = sq2[:, 0:nblk * 512]
            E("scalar", lambda e: e.activation(sqv, raw_ap, Act.Square))
            sm = sbar()
            prev_s, prev_v = 0, 0
            for b_ in range(nblk):
                ssb = bank(reg)[0:1, 0:512]
                w1 = [("s", sm)] if b_ == 0 else [("s", prev_s)]
                E("tensor", lambda e, b_=b_: e.matmul(
                    ssb, ones_b[:], sqv[:, b_ * 512:b_ * 512 + 512],
                    start=True, stop=True), wait=w1, inc=[("t", 1)])
                tm = C("t")
                E("scalar", lambda e: e.activation(rms_r[:], ssb, Act.Ln,
                                                   scale=1.0 / 128,
                                                   bias=eps8[0:1, :]),
                  wait=[("t", tm)])
                E("scalar", lambda e: e.drain())
                E("scalar", lambda e: e.activation(rms_r[:], rms_r[:], Act.Exp,
                                                   scale=-0.5))
                prev_s = sm2 = sbar()
                rbp = bank(reg + 1)[:, 0:512]
                w2 = [("s", sm2)]
                if prev_v:
                    w2.append(("v", prev_v))
                E("tensor", lambda e: e.matmul(rbp, ones_f[:], rms_r[:],
                                               start=True, stop=True),
                  wait=w2, inc=[("t", 1)])
                tm2 = C("t")
                E("vector", lambda e, b_=b_, gam=gam, rbp=rbp:
                  e.scalar_tensor_tensor(
                      out_ap[:, b_ * 512:b_ * 512 + 512], rbp, gam[:],
                      raw_ap[:, b_ * 512:b_ * 512 + 512],
                      op0=Alu.mult, op1=Alu.mult), wait=[("t", tm2)])
                prev_v = vbar()
            return prev_v

        slab_t = {}
        rms_done = [0, 0]
        for hg in range(8):
            reg = (hg % 2) * 4
            for kc in range(16):
                bu = kc % 2
                wld = ([xT_done_v, ("wc", wcast_done)]
                       if (hg == 0 and kc < 2) else [])
                if (hg, kc) != (0, 0) and slab_t.get(bu):
                    wld.append(("t", slab_t[bu]))
                lm = DMA("sync", lambda e, hg=hg, kc=kc, bu=bu: e.dma_start(
                    out=wqbuf[bu][:],
                    in_=wqb_h[kc * 128:kc * 128 + 128, hg * 256:hg * 256 + 256]),
                    wait=wld)
                wmm = [lm]
                if kc == 0 and rms_done[hg % 2]:
                    wmm.append(("v", rms_done[hg % 2]))
                for hh in range(2):
                    for th in range(2):
                        E("tensor", lambda e, kc=kc, bu=bu, hh=hh, th=th, reg=reg:
                          e.matmul(bank(reg + hh * 2 + th),
                                   wqbuf[bu][:, hh * 128:hh * 128 + 128],
                                   xqT[:, kc, th * 512:th * 512 + 512],
                                   start=(kc == 0), stop=(kc == 15)),
                          wait=wmm if (hh == 0 and th == 0) else (),
                          inc=[("t", 1)] if (hh == 1 and th == 1) else ())
                slab_t[bu] = C("t")
            qt = C("t")
            for hh in range(2):
                E("scalar", lambda e, hh=hh, reg=reg: e.copy(
                    qraw[:, hh, :], bank(reg + hh * 2, 2)),
                  wait=[("t", qt)] if hh == 0 else ())
            sbar()
            for hh in range(2):
                rm = rms_apply(qraw[:, hh, :], qhat[:, hg * 2 + hh, :], 2, reg, qg_sb)
            rms_done[hg % 2] = rm
        k_rms_done = [rms_done[0], rms_done[1]]
        for kv in range(KVH):
            reg = (kv % 2) * 4
            for kc in range(16):
                bu = kc % 2
                wld = [("t", slab_t[bu])] if slab_t.get(bu) else []
                if kv == 0 and kc == 0:
                    wld.append(("wc", wcast_done))
                lm = DMA("sync", lambda e, kv=kv, kc=kc, bu=bu: e.dma_start(
                    out=wqbuf[bu][:, 0:128],
                    in_=wkb_h[kc * 128:kc * 128 + 128, kv * 128:kv * 128 + 128]),
                    wait=wld)
                wmm = [lm]
                if kc == 0 and k_rms_done[kv % 2]:
                    wmm.append(("v", k_rms_done[kv % 2]))
                for th in range(4):
                    E("tensor", lambda e, kc=kc, bu=bu, th=th, reg=reg: e.matmul(
                        bank(reg + th), wqbuf[bu][:, 0:128],
                        xfT[:, kc, th * 512:th * 512 + 512],
                        start=(kc == 0), stop=(kc == 15)),
                      wait=wmm if th == 0 else (),
                      inc=[("t", 1)] if th == 3 else ())
                slab_t[bu] = C("t")
            kt = C("t")
            E("scalar", lambda e, reg=reg: e.copy(qraw[:, 0, :], bank(reg, 2)),
              wait=[("t", kt)])
            E("scalar", lambda e, reg=reg: e.copy(qraw[:, 1, :], bank(reg + 2, 2)))
            sbar()
            rm = rms_apply(kraw, khat[:, kv, :], 4, reg, kg_sb)
            k_rms_done[kv % 2] = rm
        kv_done = C("v")
        for kc in range(16):
            DMA("sync", lambda e, kc=kc: e.dma_start(
                out=wv_sb[:, kc, :], in_=wvb_h[kc * 128:kc * 128 + 128, :]),
                wait=[("v", kv_done), ("wc", wcast_done)] if kc == 0 else ())
        wv_lms = bld.dma_totals("sync")
        v_ev = [0, 0]
        for tt in range(16):
            reg = tt % 2
            wmm = (wv_lms if tt == 0 else []) + [("v", kv_done)]
            if v_ev[reg]:
                wmm.append(("v", v_ev[reg]))
            for kc in range(16):
                E("tensor", lambda e, tt=tt, kc=kc, reg=reg: e.matmul(
                    bank(reg), xfT[:, kc, tt * 128:tt * 128 + 128], wv_sb[:, kc, :],
                    start=(kc == 0), stop=(kc == 15)),
                  wait=wmm if kc == 0 else (),
                  inc=[("t", 1)] if kc == 15 else ())
            vt = C("t")
            v_ev[reg] = E("vector", lambda e, tt=tt, reg=reg: e.tensor_copy(
                v_sb[:, tt, :], bank(reg)), wait=[("t", vt)], inc=[("v", 1)])
        v_done = C("v")

        for kc in range(16):
            DMA("sync", lambda e, kc=kc: e.dma_start(
                out=wo_sb[:, kc, :], in_=wob_h[kc * 128:kc * 128 + 128, :]),
                wait=[("v", v_done), ("wc", wcast_done)] if kc == 0 else ())
        wo_lms = bld.dma_totals("sync")

        groups = [(qb, h) for qb in range(8) for h in range(H)]
        ng = len(groups)

        def geo(qb):
            M = MPAT[qb]
            nkt = M + 1
            ngr = (nkt + 3) // 4
            return nkt, ngr, ngr * 512

        def region(qb, h):
            ngr = geo(qb)[1]
            if ngr <= 2:
                return (h % 2) * 2
            if ngr == 3:
                return (h % 2) * 3
            return 0

        bank_guard = [0, 0, 0, 0, 0, 0]
        pt_t_by_par = [0, 0]
        pt_ev = 0
        pv_ev = 0
        pvap = ps[:, 3584:3712]
        Pbuf = [P_sb, sq2]

        def emit_S(i):
            qb, h = groups[i]
            kv = h // 4
            nkt, ngr, L5 = geo(qb)
            r0 = region(qb, h)
            w = []
            gmax = max(bank_guard[r0:r0 + ngr])
            if gmax:
                w.append(("s", gmax))
            if i == 0:
                w = [("v", v_done)] + const_ms + w
            for g in range(ngr):
                E("tensor", lambda e, qb=qb, h=h, kv=kv, g=g, r0=r0: e.matmul(
                    bank(r0 + g), qhat[:, h, qb * 128:qb * 128 + 128],
                    khat[:, kv, g * 512:g * 512 + 512], start=True, stop=True),
                  wait=w if g == 0 else (),
                  inc=[("t", 1)] if g == ngr - 1 else ())
            return C("t")

        tS_ms = emit_S(0)
        for i in range(ng):
            qb, h = groups[i]
            kv = h // 4
            nkt, ngr, L5 = geo(qb)
            r0 = region(qb, h)
            Sap = ps[:, r0 * 512:r0 * 512 + L5]
            Pv = Pbuf[i % 2]
            E("vector", lambda e, qb=qb, Sap=Sap, L5=L5: e.tensor_tensor(
                Sap[:, L5 - 512:L5], Sap[:, L5 - 512:L5], masks[:, qb, :], Alu.add),
              wait=[("t", tS_ms)])
            E("vector", lambda e: e.drain())
            E("vector", lambda e, Sap=Sap: e.tensor_reduce(
                nmax[:], Sap, axis=AX, op=Alu.max, negate=True))
            E("vector", lambda e: e.drain())
            E("vector", lambda e: e.tensor_single_scalar(
                nmax[:], nmax[:], 0.08838834764831845, Alu.mult))
            vmx = vbar()
            wexp = [("v", vmx)]
            if pt_t_by_par[i % 2]:
                wexp.append(("t", pt_t_by_par[i % 2]))
            E("scalar", lambda e, Sap=Sap, L5=L5, Pv=Pv: e.activation(
                Pv[:, 0:L5], Sap, Act.Exp, bias=nmax[:],
                scale=0.08838834764831845, accum_out=rsum[:]), wait=wexp)
            em = sbar()
            for b_ in range(r0, r0 + ngr):
                bank_guard[b_] = em
            E("vector", lambda e: e.reciprocal(rcp[:], rsum[:]), wait=[("s", em)])
            E("vector", lambda e: e.drain())
            E("vector", lambda e, L5=L5, Pv=Pv: e.tensor_scalar_mul(
                Pv[:, 0:L5], Pv[:, 0:L5], rcp[:]))
            vsc = vbar()
            if i + 1 < ng:
                tS_ms = emit_S(i + 1)
            wPT = [("v", vsc)]
            if pt_ev:
                wPT.append(("s", pt_ev))
            nw = (nkt + 7) // 8
            for w_ in range(nw):
                t0 = w_ * 8
                t1 = min(nkt, t0 + 8)
                for kt in range(t0, t1):
                    E("tensor", lambda e, kt=kt, t0=t0, Pv=Pv: e.transpose(
                        bank_bf(6)[:, (kt - t0) * 128:(kt - t0) * 128 + 128],
                        Pv[:, kt * 128:kt * 128 + 128], id_sb[:]),
                      wait=wPT if kt == t0 else (),
                      inc=[("t", 1)] if kt == t1 - 1 else ())
                tw = C("t")
                if w_ == nw - 1:
                    pt_t_by_par[i % 2] = tw
                pt_ev = E("scalar", lambda e, t0=t0, t1=t1: e.copy(
                    PT_sb[:, t0:t1, :],
                    bank_bf(6)[:, 0:(t1 - t0) * 128].rearrange(
                        "p (a b) -> p a b", b=128)),
                    wait=[("t", tw)], inc=[("s", 1)])
                wPT = [("s", pt_ev)]
            wPV = [("s", pt_ev)]
            if pv_ev:
                wPV.append(("s", pv_ev))
            for kt in range(nkt):
                E("tensor", lambda e, kt=kt, kv=kv: e.matmul(
                    pvap, v_sb[:, kt, kv * 128:kv * 128 + 128], PT_sb[:, kt, :],
                    start=(kt == 0), stop=(kt == nkt - 1)),
                  wait=wPV if kt == 0 else (),
                  inc=[("t", 1)] if kt == nkt - 1 else ())
            tpv = C("t")
            pv_ev = E("scalar", lambda e, h=h, qb=qb: e.copy(
                ao_sb[:, h, qb * 128:qb * 128 + 128], pvap),
                wait=[("t", tpv)], inc=[("s", 1)])
        a3_done_s = C("s")

        ln1_v_prev = 0
        y1t_last_t = 0
        y1t_ev = [0, 0]
        xres_used = [0, 0]
        for tc in range(8):
            bu = tc % 2
            wxr = [("s", a3_done_s)] if tc < 2 else [("v", xres_used[bu])]
            xm = DMA("sync", lambda e, tc=tc, bu=bu: e.dma_start(
                out=xres[bu][:], in_=x_q[tc * 128:tc * 128 + 128, :]),
                wait=wxr)
            wmm = [("s", a3_done_s)] + (wo_lms if tc == 0 else [])
            if ln1_v_prev:
                wmm.append(("v", ln1_v_prev))
            for og in range(4):
                for kc in range(16):
                    E("tensor", lambda e, tc=tc, og=og, kc=kc: e.matmul(
                        bank(og), ao_sb[:, kc, tc * 128:tc * 128 + 128],
                        wo_sb[:, kc, og * 512:og * 512 + 512],
                        start=(kc == 0), stop=(kc == 15)),
                      wait=wmm if (og == 0 and kc == 0) else (),
                      inc=[("t", 1)] if (og == 3 and kc == 15) else ())
            tm = C("t")
            for og in range(4):
                E("vector", lambda e, og=og, bu=bu: e.scalar_tensor_tensor(
                    y1pre[:, og * 512:og * 512 + 512],
                    xres[bu][:, og * 512:og * 512 + 512], 2.0, bank(og),
                    op0=Alu.mult, op1=Alu.add),
                  wait=[("t", tm), xm] if og == 0 else ())
            E("vector", lambda e: e.drain())
            E("vector", lambda e: e.tensor_reduce(nmean[:], y1pre[:], axis=AX,
                                                  op=Alu.add, negate=True))
            E("vector", lambda e: e.drain())
            E("vector", lambda e: e.tensor_single_scalar(
                nmean[:], nmean[:], 1.0 / 2048, Alu.mult))
            vst = vbar()
            xres_used[bu] = vst
            E("scalar", lambda e: e.activation(sq2[:, 0:2048], y1pre[:], Act.Square,
                                               bias=nmean[:], accum_out=ssd[:]),
              wait=[("v", vst)])
            E("scalar", lambda e: e.drain())
            E("scalar", lambda e: e.activation(rstd[:], ssd[:], Act.Ln,
                                               scale=1.0 / 2048, bias=eps5[:]))
            E("scalar", lambda e: e.drain())
            E("scalar", lambda e: e.activation(rstd[:], rstd[:], Act.Exp, scale=-0.5))
            sst = sbar()
            wln = [("s", sst)]
            if y1t_last_t:
                wln.append(("t", y1t_last_t))
            E("vector", lambda e: e.tensor_scalar(
                P_sb[:], y1pre[:], nmean[:], rstd[:], op0=Alu.add, op1=Alu.mult),
              wait=wln)
            ln1_v_prev = vbar()
            for bh in range(2):
                wt2 = [("v", ln1_v_prev)]
                if y1t_ev[bh]:
                    wt2.append(("v", y1t_ev[bh]))
                for kcl in range(8):
                    kc = bh * 8 + kcl
                    E("tensor", lambda e, bh=bh, kcl=kcl, kc=kc: e.transpose(
                        bank_bf(4 + bh)[0:64, kcl * 128:kcl * 128 + 128],
                        P_sb[:, kc * 128:kc * 128 + 128:2], id_sb[:]),
                      wait=wt2 if kcl == 0 else ())
                    E("tensor", lambda e, bh=bh, kcl=kcl, kc=kc: e.transpose(
                        bank_bf(4 + bh)[64:128, kcl * 128:kcl * 128 + 128],
                        P_sb[:, kc * 128 + 1:kc * 128 + 128:2], id_sb[:]),
                      inc=[("t", 1)] if kcl == 7 else ())
                tm2 = C("t")
                y1t_ev[bh] = E("vector", lambda e, bh=bh, tc=tc: e.tensor_copy(
                    y1T[:, bh * 8:bh * 8 + 8, tc * 128:tc * 128 + 128],
                    bank_bf(4 + bh)[:].rearrange("p (a b) -> p a b", b=128)),
                    wait=[("t", tm2)], inc=[("v", 1)])
            y1t_last_t = C("t")
        a4_done_v = C("v")

        gu_mm = [0, 0]
        inter_ev = [0, 0]
        dn_mm = [0, 0]
        y2_add = [0, 0]
        dn_last_t = 0
        first = True
        for sc in range(8):
            for nc2 in range(16):
                nch = sc * 16 + nc2
                par = nc2 % 2
                wld = ([("t", gu_mm[par])] if gu_mm[par]
                       else [("cc", ag_done), ("v", a4_done_v)])
                glm1 = DMA("sync", lambda e, nch=nch, par=par: e.dma_start(
                    out=gub[:, par * 2, :, :], in_=g_fu.ap()[nch]), wait=wld)
                glm2 = DMA("sync", lambda e, nch=nch, par=par: e.dma_start(
                    out=gub[:, par * 2 + 1, :, :], in_=u_fu.ap()[nch]))
                reg = par * 4
                wmm = [glm1, glm2]
                if inter_ev[par]:
                    wmm.append(("v", inter_ev[par]))
                if sc > 0 and nc2 < 2:
                    wmm += [("v", y2_add[0]), ("v", y2_add[1])]
                if sc == 0 and nc2 < 2:
                    wmm.append(("v", a4_done_v))
                for kc in range(16):
                    for th in range(2):
                        E("tensor", lambda e, par=par, kc=kc, th=th, reg=reg:
                          e.matmul(bank(reg + th), gub[:, par * 2, kc, :],
                                   y1T[:, kc, th * 512:th * 512 + 512],
                                   start=(kc == 0), stop=(kc == 15)),
                          wait=wmm if (kc == 0 and th == 0) else ())
                        E("tensor", lambda e, par=par, kc=kc, th=th, reg=reg:
                          e.matmul(bank(reg + 2 + th), gub[:, par * 2 + 1, kc, :],
                                   y1T[:, kc, th * 512:th * 512 + 512],
                                   start=(kc == 0), stop=(kc == 15)),
                          inc=[("t", 1)] if (kc == 15 and th == 1) else ())
                gu_mm[par] = C("t")
                wge = [("t", gu_mm[par])]
                if inter_ev[par]:
                    wge.append(("v", inter_ev[par]))
                E("scalar", lambda e, par=par, reg=reg: e.activation(
                    sq2[:, par * 1024:par * 1024 + 1024], bank(reg, 2),
                    Act.Gelu_apprx_tanh), wait=wge)
                sg = sbar()
                wiv = [("s", sg)]
                if dn_last_t:
                    wiv.append(("t", dn_last_t))
                inter_ev[par] = E("vector", lambda e, nc2=nc2, par=par, reg=reg:
                                  e.tensor_tensor(
                                      inter[:, nc2, :], bank(reg + 2, 2),
                                      sq2[:, par * 1024:par * 1024 + 1024], Alu.mult),
                                  wait=wiv, inc=[("v", 1)])
            iv = C("v")
            for hc in range(16):
                par = hc % 2
                wld = [("t", dn_mm[par])] if dn_mm[par] else [("cc", ag_done),
                                                             ("v", a4_done_v)]
                dlm = DMA("sync", lambda e, sc=sc, hc=hc, par=par: e.dma_start(
                    out=dnb[:, par, :, :], in_=d_fu.ap()[sc * 16 + hc]),
                    wait=wld)
                reg = par * 2
                wmm = [dlm, ("v", iv)]
                if y2_add[par]:
                    wmm.append(("v", y2_add[par]))
                for nc2 in range(16):
                    for th in range(2):
                        E("tensor", lambda e, par=par, nc2=nc2, th=th, reg=reg:
                          e.matmul(bank(reg + th), dnb[:, par, nc2, :],
                                   inter[:, nc2, th * 512:th * 512 + 512],
                                   start=(nc2 == 0), stop=(nc2 == 15)),
                          wait=wmm if (nc2 == 0 and th == 0) else (),
                          inc=[("t", 1)] if (nc2 == 15 and th == 1) else ())
                dn_mm[par] = C("t")
                if first and hc < 2:
                    y2_add[par] = E("vector", lambda e, hc=hc, reg=reg: e.tensor_copy(
                        y2acc[:, hc, :], bank(reg, 2)),
                        wait=[("t", dn_mm[par])], inc=[("v", 1)])
                elif first:
                    y2_add[par] = E("vector", lambda e, hc=hc, reg=reg: e.tensor_copy(
                        y2acc[:, hc, :], bank(reg, 2)),
                        wait=[("t", dn_mm[par]), ("v", y2_add[par])], inc=[("v", 1)])
                else:
                    y2_add[par] = E("vector", lambda e, hc=hc, reg=reg:
                                    e.scalar_tensor_tensor(
                                        y2acc[:, hc, :], bank(reg, 2), 1.0,
                                        y2acc[:, hc, :], op0=Alu.mult, op1=Alu.add),
                                    wait=[("t", dn_mm[par]), ("v", y2_add[par])],
                                    inc=[("v", 1)])
            dn_last_t = C("t")
            first = False
        ffn_done_v = C("v")

        for kc in range(16):
            E("vector", lambda e, kc=kc: e.tensor_tensor(
                y2acc[:, kc, :], y2acc[:, kc, :], y1T[:, kc, :], Alu.add),
              wait=[("v", ffn_done_v)] if kc == 0 else ())
        E("vector", lambda e: e.drain())
        for kc in range(16):
            E("vector", lambda e, kc=kc: e.tensor_copy(y1T[:, kc, :], y2acc[:, kc, :]))
        vy2b = vbar()
        E("scalar", lambda e: e.activation(ar2[:], ar3[:], Act.Square),
          wait=[("v", vy2b)])
        ssq = sbar()
        st_ms = [0, 0]
        vvar_prev = 0
        for tb in range(2):
            w0 = [("s", ssq)] if tb == 0 else [("v", vvar_prev)]
            for kc in range(16):
                E("tensor", lambda e, tb=tb, kc=kc: e.matmul(
                    bank(0)[0:1, 0:512], ones_b[:],
                    y1T[:, kc, tb * 512:tb * 512 + 512],
                    start=(kc == 0), stop=(kc == 15)),
                  wait=w0 if kc == 0 else ())
            for kc in range(16):
                E("tensor", lambda e, tb=tb, kc=kc: e.matmul(
                    bank(1)[0:1, 0:512], ones_b[:],
                    inter[:, kc, tb * 512:tb * 512 + 512],
                    start=(kc == 0), stop=(kc == 15)),
                  inc=[("t", 1)] if kc == 15 else ())
            tsum = C("t")
            E("vector", lambda e: e.tensor_single_scalar(
                ln2a[:], bank(0)[0:1, 0:512], -1.0 / 2048, Alu.mult),
              wait=[("t", tsum)])
            E("vector", lambda e: e.tensor_single_scalar(
                ln2b[:], bank(1)[0:1, 0:512], 1.0 / 2048, Alu.mult))
            E("vector", lambda e: e.drain())
            E("vector", lambda e: e.tensor_tensor(ln2c[:], ln2a[:], ln2a[:], Alu.mult))
            E("vector", lambda e: e.drain())
            E("vector", lambda e: e.tensor_tensor(ln2b[:], ln2b[:], ln2c[:],
                                                  Alu.subtract))
            vvar_prev = vvar = vbar()
            E("scalar", lambda e: e.activation(ln2c[:], ln2b[:], Act.Ln,
                                               bias=eps5[0:1, :]),
              wait=[("v", vvar)])
            E("scalar", lambda e: e.drain())
            E("scalar", lambda e: e.activation(ln2c[:], ln2c[:], Act.Exp, scale=-0.5))
            srs = sbar()
            wbc = [("s", srs)]
            if st_ms[0]:
                wbc.append(("v", st_ms[0]))
            E("tensor", lambda e: e.matmul(bank(2), ones_f[:], ln2a[:],
                                           start=True, stop=True), wait=wbc)
            E("tensor", lambda e: e.matmul(bank(3), ones_f[:], ln2c[:],
                                           start=True, stop=True), inc=[("t", 1)])
            tbc = C("t")
            for kc in range(16):
                E("vector", lambda e, tb=tb, kc=kc: e.tensor_tensor(
                    y2acc[:, kc, tb * 512:tb * 512 + 512],
                    y2acc[:, kc, tb * 512:tb * 512 + 512], bank(2), Alu.add),
                  wait=[("t", tbc)] if kc == 0 else ())
            E("vector", lambda e: e.drain())
            for kc in range(16):
                E("vector", lambda e, tb=tb, kc=kc: e.tensor_tensor(
                    y2acc[:, kc, tb * 512:tb * 512 + 512],
                    y2acc[:, kc, tb * 512:tb * 512 + 512], bank(3), Alu.mult))
            st_ms[tb] = vbar()
        ev_fin = 0
        st_fin = [0, 0]
        for tc in range(8):
            bu = tc % 2
            wt = [("v", st_ms[tc // 4])]
            if ev_fin:
                wt.append(("v", ev_fin))
            for kc in range(16):
                bk = 4 + kc // 4
                E("tensor", lambda e, tc=tc, kc=kc, bk=bk: e.transpose(
                    bank(bk)[:, (kc % 4) * 128:(kc % 4) * 128 + 128],
                    y2acc[:, kc, tc * 128:tc * 128 + 128], idf_sb[:]),
                  wait=wt if kc == 0 else (),
                  inc=[("t", 1)] if kc == 15 else ())
            tfin = C("t")
            wev = [("t", tfin)]
            if st_fin[bu]:
                wev.append(st_fin[bu])
            for bk in range(4):
                E("vector", lambda e, bu=bu, bk=bk: e.tensor_copy(
                    y2tok[bu][:, bk * 512:bk * 512 + 512].rearrange(
                        "p (a j two) -> p a two j", a=4, two=2),
                    bank(4 + bk)[:].rearrange("p (a b) -> p a b", b=128).rearrange(
                        "p a (two j) -> p a two j", two=2)),
                  wait=wev if bk == 0 else ())
            ev_fin = vbar()
            st_fin[bu] = DMA("gpsimd", lambda e, tc=tc, bu=bu: e.dma_start(
                out=y_h[tc * 128:tc * 128 + 128, :], in_=y2tok[bu][:]),
                wait=[("v", ev_fin)])
        E("gpsimd", lambda e: e.nop(),
          wait=bld.dma_totals("gpsimd") + bld.dma_totals("sync"))

        for ename in ENGS:
            entries = bld.entries[ename]

            def runner(eng, _entries=entries):
                for wait, f, incs in _entries:
                    for s_, v_ in wait:
                        eng.wait_ge(SEM[s_], v_)
                    ins = f(eng)
                    for s_, n_ in incs:
                        ins.then_inc(SEM[s_], n_)
            getattr(blk, ename)(runner)

    n_inst = sum(len(v) for v in bld.entries.values())
    print(f"[kernel] emitted {n_inst} entries "
          + str({k: len(v) for k, v in bld.entries.items()}))
    return nc


def kernel(**inputs):
    x = np.asarray(inputs["L0_attn_x"], dtype=np.float32)
    wq = np.asarray(inputs["L0_attn_Wq"], dtype=np.float32)
    wk = np.asarray(inputs["L0_attn_Wk"], dtype=np.float32)
    wv = np.asarray(inputs["L0_attn_Wv"], dtype=np.float32)
    wo = np.asarray(inputs["L0_attn_Wo"], dtype=np.float32)
    qg = np.ascontiguousarray(
        np.asarray(inputs["L0_attn_q_gamma"], dtype=np.float32).reshape(128, 1))
    kg = np.ascontiguousarray(
        np.asarray(inputs["L0_attn_k_gamma"], dtype=np.float32).reshape(128, 1))
    gp = np.asarray(inputs["L0_ffn_gate_W_packed"], dtype=np.int32)
    gs = np.asarray(inputs["L0_ffn_gate_scales"], dtype=np.float32)
    up = np.asarray(inputs["L0_ffn_up_W_packed"], dtype=np.int32)
    us = np.asarray(inputs["L0_ffn_up_scales"], dtype=np.float32)
    dp = np.asarray(inputs["L0_ffn_down_W_packed"], dtype=np.int32)
    ds = np.asarray(inputs["L0_ffn_down_scales"], dtype=np.float32)

    ident = np.eye(128, dtype=ml_dtypes.bfloat16)
    identf = np.eye(128, dtype=np.float32)

    in_maps = []
    for c in range(NCORES):
        b, par = c // 2, c % 2
        blocks = QBLOCKS[par]
        xq_rows = np.concatenate([x[b, g * 128:(g + 1) * 128, :] for g in blocks], 0)
        mk = np.zeros((8, 128, 512), dtype=np.float32)
        for qb, g in enumerate(blocks):
            L5 = ((MPAT[qb] + 1 + 3) // 4) * 512
            k0 = L5 - 512
            kk = k0 + np.arange(512)[None, :]
            qq = g * 128 + np.arange(128)[:, None]
            mk[qb] = np.where(kk <= qq, 0.0, NEG)
        in_maps.append({
            "x_full": np.ascontiguousarray(x[b]),
            "x_q": np.ascontiguousarray(xq_rows),
            "wq": wq, "wk": wk, "wv": wv, "wo": wo,
            "gp": np.ascontiguousarray(gp[c * 2048:(c + 1) * 2048]),
            "gs": np.ascontiguousarray(gs[c * 2048:(c + 1) * 2048]),
            "up": np.ascontiguousarray(up[c * 2048:(c + 1) * 2048]),
            "us": np.ascontiguousarray(us[c * 2048:(c + 1) * 2048]),
            "dp": np.ascontiguousarray(dp[:, c * 1024:(c + 1) * 1024]),
            "ds": np.ascontiguousarray(ds[:, c * 64:(c + 1) * 64]),
            "qg": qg, "kg": kg, "ident": ident, "identf": identf, "maskqb": mk,
        })

    nc = build_nc()
    res = run_bass_kernel_spmd(nc, in_maps, core_ids=list(range(NCORES)))

    out = np.empty((B, S, D), dtype=np.float32)
    for c in range(NCORES):
        b, par = c // 2, c % 2
        y = res.results[c]["y"]
        for qb, g in enumerate(QBLOCKS[par]):
            out[b, g * 128:(g + 1) * 128, :] = y[qb * 128:(qb + 1) * 128, :]
    return out


# version 08
# speedup vs baseline: 1.0485x; 1.0485x over previous
import contextlib
import numpy as np
import ml_dtypes
import concourse.bass as bass
import concourse.mybir as mybir
from concourse.bass_utils import run_bass_kernel_spmd

F32 = mybir.dt.float32
BF16 = mybir.dt.bfloat16
I32 = mybir.dt.int32
I16 = mybir.dt.int16
Alu = mybir.AluOpType
Act = mybir.ActivationFunctionType
AX = mybir.AxisListType.X

B, S, D = 4, 2048, 2048
H, KVH, DH = 16, 4, 128
NCORES = 8
NEG = -1e30
TQ = 1024

QBLOCKS = [[0, 3, 4, 7, 8, 11, 12, 15], [1, 2, 5, 6, 9, 10, 13, 14]]
MPAT = [1, 3, 5, 7, 9, 11, 13, 15]

ENGS = ("sync", "tensor", "vector", "scalar", "gpsimd")


NSLOT = 8


class Build:

    def __init__(self):
        self.entries = {e: [] for e in ENGS}
        self.cnt = {}
        self.slot_i = {"sync": 0, "gpsimd": 0}

    def emit(self, eng, f, wait=(), inc=()):
        self.entries[eng].append((tuple(wait), f, tuple(inc)))
        out = None
        for s, n in inc:
            self.cnt[s] = self.cnt.get(s, 0) + n
            out = self.cnt[s]
        return out

    def dma(self, eng, f, wait=()):
        q = "ld" if eng == "sync" else "gd"
        i = self.slot_i[eng] % NSLOT
        self.slot_i[eng] += 1
        s = f"{q}{i}"
        uses = self.cnt.get(s, 0)
        w = list(wait)
        if uses:
            w.append((s, uses))
        v = self.emit(eng, f, wait=w, inc=[(s, 16)])
        return (s, v)

    def dma_totals(self, eng):
        q = "ld" if eng == "sync" else "gd"
        return [(f"{q}{i}", self.cnt.get(f"{q}{i}", 0)) for i in range(NSLOT)
                if self.cnt.get(f"{q}{i}", 0)]

    def cur(self, s):
        return self.cnt.get(s, 0)


def build_nc():
    nc = bass.Bass("TRN2", target_bir_lowering=False, debug=False)
    KJ, NB = 1024, 64

    x_full = nc.dram_tensor("x_full", [S, D], F32, kind="ExternalInput").ap()
    x_q = nc.dram_tensor("x_q", [TQ, D], F32, kind="ExternalInput").ap()
    wq_h = nc.dram_tensor("wq", [D, D], F32, kind="ExternalInput").ap()
    wk_h = nc.dram_tensor("wk", [D, KVH * DH], F32, kind="ExternalInput").ap()
    wv_h = nc.dram_tensor("wv", [D, KVH * DH], F32, kind="ExternalInput").ap()
    wo_h = nc.dram_tensor("wo", [D, D], F32, kind="ExternalInput").ap()
    gp_h = nc.dram_tensor("gp", [2048, KJ], I32, kind="ExternalInput").ap()
    gs_h = nc.dram_tensor("gs", [2048, NB], F32, kind="ExternalInput").ap()
    up_h = nc.dram_tensor("up", [2048, KJ], I32, kind="ExternalInput").ap()
    us_h = nc.dram_tensor("us", [2048, NB], F32, kind="ExternalInput").ap()
    dp_h = nc.dram_tensor("dp", [2048, KJ], I32, kind="ExternalInput").ap()
    ds_h = nc.dram_tensor("ds", [2048, NB], F32, kind="ExternalInput").ap()
    qg_h = nc.dram_tensor("qg", [128, 1], F32, kind="ExternalInput").ap()
    kg_h = nc.dram_tensor("kg", [128, 1], F32, kind="ExternalInput").ap()
    id_h = nc.dram_tensor("ident", [128, 128], BF16, kind="ExternalInput").ap()
    idf_h = nc.dram_tensor("identf", [128, 128], F32, kind="ExternalInput").ap()
    mk_h = nc.dram_tensor("maskqb", [8, 128, 512], F32, kind="ExternalInput").ap()
    y_h = nc.dram_tensor("y", [TQ, D], F32, kind="ExternalOutput").ap()

    wqb_h = nc.dram_tensor("wq_bf", [D, D], BF16).ap()
    wkb_h = nc.dram_tensor("wk_bf", [D, KVH * DH], BF16).ap()
    wvb_h = nc.dram_tensor("wv_bf", [D, KVH * DH], BF16).ap()
    wob_h = nc.dram_tensor("wo_bf", [D, D], BF16).ap()
    g_sh = nc.dram_tensor("g_sh", [16, 128, 16, 128], BF16)
    u_sh = nc.dram_tensor("u_sh", [16, 128, 16, 128], BF16)
    d_sh = nc.dram_tensor("d_sh", [16, 128, 16, 128], BF16)
    g_fu = nc.dram_tensor("g_fu", [128, 128, 16, 128], BF16)
    u_fu = nc.dram_tensor("u_fu", [128, 128, 16, 128], BF16)
    d_fu = nc.dram_tensor("d_fu", [128, 128, 16, 128], BF16)

    bld = Build()
    E, C = bld.emit, bld.cur

    def vbar():
        return E("vector", lambda e: e.drain(), inc=[("v", 1)])

    def sbar():
        return E("scalar", lambda e: e.drain(), inc=[("s", 1)])

    with contextlib.ExitStack() as ST:
        alloc = ST.enter_context

        ar1 = alloc(nc.sbuf_tensor("ar1", [128, 32768], BF16))
        ar2 = alloc(nc.sbuf_tensor("ar2", [128, 16384], BF16))
        ar3 = alloc(nc.sbuf_tensor("ar3", [128, 16384], BF16))
        ar4 = alloc(nc.sbuf_tensor("ar4", [128, 8192], BF16))
        ar5 = alloc(nc.sbuf_tensor("ar5", [128, 8192], BF16))
        ar6 = alloc(nc.sbuf_tensor("ar6", [128, 8192], BF16))
        ar7 = alloc(nc.sbuf_tensor("ar7", [128, 8192], BF16))
        arp = alloc(nc.sbuf_tensor("arp", [128, 4096], BF16))
        id_sb = alloc(nc.sbuf_tensor("id_sb", [128, 128], BF16))
        idf_sb = alloc(nc.sbuf_tensor("idf_sb", [128, 128], F32))
        qg_sb = alloc(nc.sbuf_tensor("qg_sb", [128, 1], F32))
        kg_sb = alloc(nc.sbuf_tensor("kg_sb", [128, 1], F32))
        ones_b = alloc(nc.sbuf_tensor("ones_b", [128, 1], BF16))
        ones_f = alloc(nc.sbuf_tensor("ones_f", [1, 128], F32))
        nmax = alloc(nc.sbuf_tensor("nmax", [128, 1], F32))
        rsum = alloc(nc.sbuf_tensor("rsum", [128, 1], F32))
        rcp = alloc(nc.sbuf_tensor("rcp", [128, 1], F32))
        nmean = alloc(nc.sbuf_tensor("nmean", [128, 1], F32))
        ssd = alloc(nc.sbuf_tensor("ssd", [128, 1], F32))
        rstd = alloc(nc.sbuf_tensor("rstd", [128, 1], F32))
        sq2 = alloc(nc.sbuf_tensor("sq2", [128, 2048], BF16))
        eps8 = alloc(nc.sbuf_tensor("eps8", [128, 1], F32))
        eps5 = alloc(nc.sbuf_tensor("eps5", [128, 1], F32))
        ps = alloc(nc.psum_tensor("ps", [128, 4096], F32))

        SEM = {}
        semnames = ["t", "v", "s", "cc", "wc"]
        semnames += [f"ld{i}" for i in range(NSLOT)] + [f"gd{i}" for i in range(NSLOT)]
        for s_ in semnames:
            SEM[s_] = alloc(nc.semaphore(s_))
        blk = alloc(nc.Block())

        def bank(i, n=1):
            return ps[:, 512 * i:512 * (i + n)]

        def bank_bf(i, n=1):
            return ps[:, 512 * i:512 * (i + n)].bitcast(BF16)

        xfT = ar1[:].rearrange("p (a b) -> p a b", b=2048)
        wo_sb = ar1[:].rearrange("p (a b) -> p a b", b=2048)
        y2acc = ar1[:].bitcast(F32).rearrange("p (a b) -> p a b", b=1024)
        xqT = ar2[:].rearrange("p (a b) -> p a b", b=1024)
        wv_sb = ar2[:, 8192:16384].rearrange("p (a b) -> p a b", b=512)
        ao_sb = ar2[:].rearrange("p (a b) -> p a b", b=1024)
        inter = ar2[:].rearrange("p (a b) -> p a b", b=1024)
        qhat = ar3[:].rearrange("p (a b) -> p a b", b=1024)
        y1T = ar3[:].rearrange("p (a b) -> p a b", b=1024)
        khat = ar4[:].rearrange("p (a b) -> p a b", b=2048)
        gub = ar4[:].rearrange("p (a b c) -> p a b c", b=16, c=128)
        v_sb = ar5[:].rearrange("p (a b) -> p a b", b=512)
        dnb = ar5[:, 0:4096].rearrange("p (a b c) -> p a b c", b=16, c=128)
        y1pre = ar5[:, 0:4096].bitcast(F32)
        dq_b16 = [ar1[:, 0:1024].bitcast(I16), ar1[:, 1024:2048].bitcast(I16)]
        dq_m = ar1[:, 2048:3072].bitcast(I16)
        dq_m2 = ar1[:, 3072:4096].bitcast(I16)
        dq_a1 = ar1[:, 4096:5120].bitcast(I16)
        dq_a1h = ar1[:, 5120:6144].bitcast(I16)
        dq_bb = ar1[:, 6144:7168].bitcast(I16)
        dq_bbh = ar1[:, 7168:8192].bitcast(I16)
        dq_s2 = ar1[:, 8192:9216].bitcast(I16)
        dq_s2h = ar1[:, 9216:10240].bitcast(I16)
        dq_bits = [ar1[:, 10240:12288].bitcast(I16).rearrange("p (a b) -> p a b", b=1024),
                   ar1[:, 12288:14336].bitcast(I16).rearrange("p (a b) -> p a b", b=1024)]
        dq_w = [ar1[:, 10240:12288].rearrange("p (a b) -> p a b", b=1024),
                ar1[:, 12288:14336].rearrange("p (a b) -> p a b", b=1024)]
        dq_sc = [ar1[:, 14336:14464].bitcast(F32), ar1[:, 14464:14592].bitcast(F32)]
        dq_st = [ar1[:, 14592:16640].rearrange("p (a b) -> p a b", b=128),
                 ar1[:, 16640:18688].rearrange("p (a b) -> p a b", b=128)]
        masks = ar7[:].bitcast(F32).rearrange("p (a b) -> p a b", b=512)
        xtok = [ar6[:, 0:2048], ar6[:, 2048:4096]]
        wqbuf = [ar6[:, 4096:4352], ar6[:, 4352:4608]]
        kraw = ar6[:, 5632:7680]
        qraw = kraw.rearrange("p (a b) -> p a b", b=1024)
        xres = [ar6[:, 0:4096].bitcast(F32), ar6[:, 4096:8192].bitcast(F32)]
        y2tok = [ar6[:, 0:4096].bitcast(F32), ar6[:, 4096:8192].bitcast(F32)]
        rms_r = ar6[:, 4608:5632].bitcast(F32)[0:1, 0:512]
        ln2a = arp[:, 0:1024].bitcast(F32)[0:1, 0:512]
        ln2b = arp[:, 1024:2048].bitcast(F32)[0:1, 0:512]
        ln2c = arp[:, 2048:3072].bitcast(F32)[0:1, 0:512]
        P_sb = arp[:, 0:2048]
        PT_sb = arp[:, 2048:4096].rearrange("p (a b) -> p a b", b=128)

        DMA = bld.dma
        id_ms = DMA("sync", lambda e: e.dma_start(out=id_sb[:], in_=id_h))
        idf_ms = DMA("sync", lambda e: e.dma_start(out=idf_sb[:], in_=idf_h))
        qg_ms = DMA("sync", lambda e: e.dma_start(out=qg_sb[:], in_=qg_h))
        kg_ms = DMA("sync", lambda e: e.dma_start(out=kg_sb[:], in_=kg_h))
        for qb in range(8):
            DMA("sync", lambda e, qb=qb: e.dma_start(out=masks[:, qb, :], in_=mk_h[qb]))
        const_ms = bld.dma_totals("sync")
        E("vector", lambda e: e.memset(ones_b[:], 1.0))
        E("vector", lambda e: e.memset(ones_f[:], 1.0))
        E("vector", lambda e: e.memset(eps8[:], 1e-8))
        E("vector", lambda e: e.memset(eps5[:], 1e-5))
        E("vector", lambda e: e.drain())

        def dq_chain(v, bu):
            b16v = dq_b16[bu]
            bits = dq_bits[bu]
            wv = dq_w[bu]
            v.tensor_single_scalar(dq_m[:], b16v[:], 7, Alu.bitwise_and)
            v.tensor_single_scalar(dq_m2[:], b16v[:], 0x70, Alu.bitwise_and)
            v.tensor_single_scalar(dq_s2[:], b16v[:], 8, Alu.bitwise_and)
            v.tensor_single_scalar(dq_s2h[:], b16v[:], 128, Alu.bitwise_and)
            v.drain()
            v.tensor_single_scalar(dq_a1[:], dq_m[:], 64, Alu.mult)
            v.tensor_scalar(dq_bb[:], dq_m[:], 2, 16128, op0=Alu.min, op1=Alu.mult)
            v.tensor_single_scalar(dq_a1h[:], dq_m2[:], 4, Alu.mult)
            v.tensor_scalar(dq_bbh[:], dq_m2[:], 32, 1008, op0=Alu.min, op1=Alu.mult)
            v.tensor_single_scalar(dq_s2[:], dq_s2[:], -4096, Alu.mult)
            v.tensor_single_scalar(dq_s2h[:], dq_s2h[:], -256, Alu.mult)
            v.drain()
            v.scalar_tensor_tensor(dq_a1[:], dq_a1[:], 16128.0, dq_bb[:],
                                   op0=Alu.add, op1=Alu.min)
            v.scalar_tensor_tensor(dq_a1h[:], dq_a1h[:], 16128.0, dq_bbh[:],
                                   op0=Alu.add, op1=Alu.min)
            v.drain()
            v.tensor_tensor(bits[:, 0, :], dq_a1[:], dq_s2[:], Alu.add)
            v.tensor_tensor(bits[:, 1, :], dq_a1h[:], dq_s2h[:], Alu.add)
            v.drain()
            sc_b = dq_sc[bu][:].unsqueeze(2).broadcast_to([128, 64, 16])
            for pl in range(2):
                bits_bf = bits[:, pl, :].bitcast(BF16).rearrange(
                    "n (b u) -> n b u", u=16)
                v.tensor_tensor(wv[:, pl, :].rearrange("n (b u) -> n b u", u=16),
                                bits_bf, sc_b, Alu.mult)
            return v.drain()

        dq_cons = [0, 0]
        tr_done = [0, 0]
        evac_ms = [0, 0]
        stage_ms = [0, 0]
        for pk, scl, sh in ((gp_h, gs_h, g_sh), (up_h, us_h, u_sh),
                            (dp_h, ds_h, d_sh)):
            for oc in range(16):
                bu = oc % 2
                wld = [("v", dq_cons[bu])] if dq_cons[bu] else []
                m1 = DMA("gpsimd", lambda e, pk=pk, oc=oc, bu=bu: e.dma_start(
                    out=dq_b16[bu][:],
                    in_=pk[oc * 128:oc * 128 + 128, :].rearrange(
                        "(j two) c -> two j c", two=2)), wait=wld)
                m2 = DMA("sync", lambda e, scl=scl, oc=oc, bu=bu: e.dma_start(
                    out=dq_sc[bu][:],
                    in_=scl[oc * 128:oc * 128 + 128, :].rearrange(
                        "(j two) c -> two j c", two=2)), wait=wld)
                wv_ = [m1, m2]
                if tr_done[bu]:
                    wv_.append(("t", tr_done[bu]))
                E("vector", lambda e, bu=bu: dq_chain(e, bu), wait=wv_)
                dq_cons[bu] = vm = E("vector", lambda e: e.drain(), inc=[("v", 1)])
                for bh in range(2):
                    wt = [("v", vm)]
                    if evac_ms[bh]:
                        wt.append(("s", evac_ms[bh]))
                    for kcl in range(8):
                        kc = bh * 8 + kcl
                        E("tensor", lambda e, bh=bh, kcl=kcl, kc=kc, bu=bu: e.transpose(
                            bank_bf(6 + bh)[0:64, kcl * 128:kcl * 128 + 128],
                            dq_w[bu][:, 0, kc * 64:kc * 64 + 64], id_sb[:]),
                          wait=wt if kcl == 0 else ())
                        E("tensor", lambda e, bh=bh, kcl=kcl, kc=kc, bu=bu: e.transpose(
                            bank_bf(6 + bh)[64:128, kcl * 128:kcl * 128 + 128],
                            dq_w[bu][:, 1, kc * 64:kc * 64 + 64], id_sb[:]),
                          inc=[("t", 1)] if kcl == 7 else ())
                    tm = C("t")
                    wev = [("t", tm)]
                    if stage_ms[bu]:
                        wev.append(stage_ms[bu])
                    evac_ms[bh] = E("scalar", lambda e, bh=bh, bu=bu: e.copy(
                        dq_st[bu][:, bh * 8:bh * 8 + 8, :],
                        bank_bf(6 + bh)[:].rearrange("p (a b) -> p a b", b=128)),
                        wait=wev, inc=[("s", 1)])
                tr_done[bu] = tm
                stage_ms[bu] = DMA("sync", lambda e, sh=sh, oc=oc, bu=bu: e.dma_start(
                    out=sh.ap()[oc], in_=dq_st[bu][:]),
                    wait=[("s", evac_ms[1])])
        dq_store_last = [stage_ms[0], stage_ms[1]]
        for src, dst in ((wq_h, wqb_h), (wk_h, wkb_h), (wv_h, wvb_h), (wo_h, wob_h)):
            E("gpsimd", lambda e, s_=src, d_=dst: e.dma_start(out=d_, in_=s_),
              inc=[("wc", 16)])
        wcast_done = C("wc")
        ag_w = bld.dma_totals("gpsimd") + bld.dma_totals("sync")
        for sh_, fu_ in ((g_sh, g_fu), (u_sh, u_fu), (d_sh, d_fu)):
            E("gpsimd", lambda e, sh_=sh_, fu_=fu_: e.collective_compute(
                "AllGather", Alu.bypass, replica_groups=[list(range(NCORES))],
                ins=[sh_.ap().opt()], outs=[fu_.ap().opt()]),
              wait=ag_w, inc=[("cc", 1)])
            ag_w = ()
        ag_done = C("cc")

        xa_ms = [0, 0]
        ev45 = [0, 0]
        for which, (ntt, dst) in enumerate(((16, xfT), (8, xqT))):
            xd = x_full if which == 0 else x_q
            for tt in range(ntt):
                bu = tt % 2
                wt0 = [("t", xa_ms[bu])] if xa_ms[bu] else []
                gm = DMA("gpsimd", lambda e, xd=xd, tt=tt, bu=bu: e.dma_start(
                    out=xtok[bu][:], in_=xd[tt * 128:tt * 128 + 128, :]),
                    wait=wt0)
                for bh in range(2):
                    wt2 = [gm]
                    if ev45[bh]:
                        wt2.append(("s", ev45[bh]))
                    for kcl in range(8):
                        kc = bh * 8 + kcl
                        E("tensor", lambda e, bu=bu, bh=bh, kcl=kcl, kc=kc: e.transpose(
                            bank_bf(4 + bh)[:, kcl * 128:kcl * 128 + 128],
                            xtok[bu][:, kc * 128:kc * 128 + 128], id_sb[:]),
                          wait=wt2 if kcl == 0 else (),
                          inc=[("t", 1)] if kcl == 7 else ())
                    tm = C("t")
                    if bh == 1:
                        xa_ms[bu] = tm
                    wv2 = [("t", tm)]
                    if which == 0 and tt == 0 and bh == 0:
                        wv2 += dq_store_last
                    ev45[bh] = E("scalar", lambda e, dst=dst, bh=bh, tt=tt: e.copy(
                        dst[:, bh * 8:bh * 8 + 8, tt * 128:tt * 128 + 128],
                        bank_bf(4 + bh)[:].rearrange("p (a b) -> p a b", b=128)),
                        wait=wv2, inc=[("s", 1)])
        xT_done_v = ("s", C("s"))

        rms_ln = arp[:].bitcast(F32)[0:1, 0:2048]

        def rms_apply(raw_ap, out_aps, nblk, reg, gam):
            sqv = sq2[:, 0:nblk * 512]
            E("scalar", lambda e: e.activation(sqv, raw_ap, Act.Square))
            sm = sbar()
            for b_ in range(nblk):
                E("tensor", lambda e, b_=b_: e.matmul(
                    ps[0:1, (reg + b_) * 512:(reg + b_) * 512 + 512], ones_b[:],
                    sqv[:, b_ * 512:b_ * 512 + 512], start=True, stop=True),
                  wait=[("s", sm)] if b_ == 0 else (),
                  inc=[("t", 1)] if b_ == nblk - 1 else ())
            tm = C("t")
            E("scalar", lambda e: e.activation(
                rms_ln[:, 0:nblk * 512], ps[0:1, reg * 512:(reg + nblk) * 512],
                Act.Ln, scale=1.0 / 128, bias=eps8[0:1, :]), wait=[("t", tm)])
            E("scalar", lambda e: e.drain())
            E("scalar", lambda e: e.activation(rms_ln[:, 0:nblk * 512],
                                               rms_ln[:, 0:nblk * 512], Act.Exp,
                                               scale=-0.5))
            sm2 = sbar()
            for b_ in range(nblk):
                E("tensor", lambda e, b_=b_: e.matmul(
                    ps[:, (reg + b_) * 512:(reg + b_) * 512 + 512], ones_f[:],
                    rms_ln[:, b_ * 512:b_ * 512 + 512], start=True, stop=True),
                  wait=[("s", sm2)] if b_ == 0 else (),
                  inc=[("t", 1)] if b_ == nblk - 1 else ())
            tm2 = C("t")
            for b_ in range(nblk):
                E("vector", lambda e, b_=b_, gam=gam: e.scalar_tensor_tensor(
                    out_aps[b_], ps[:, (reg + b_) * 512:(reg + b_) * 512 + 512],
                    gam[:], raw_ap[:, b_ * 512:b_ * 512 + 512],
                    op0=Alu.mult, op1=Alu.mult),
                  wait=[("t", tm2)] if b_ == 0 else ())
            return vbar()

        slab_t = {}
        rms_done = [0, 0]
        for hg in range(8):
            reg = (hg % 2) * 4
            for kc in range(16):
                bu = kc % 2
                wld = ([xT_done_v, ("wc", wcast_done)]
                       if (hg == 0 and kc < 2) else [])
                if (hg, kc) != (0, 0) and slab_t.get(bu):
                    wld.append(("t", slab_t[bu]))
                lm = DMA("sync", lambda e, hg=hg, kc=kc, bu=bu: e.dma_start(
                    out=wqbuf[bu][:],
                    in_=wqb_h[kc * 128:kc * 128 + 128, hg * 256:hg * 256 + 256]),
                    wait=wld)
                wmm = [lm]
                if kc == 0 and rms_done[hg % 2]:
                    wmm.append(("v", rms_done[hg % 2]))
                for hh in range(2):
                    for th in range(2):
                        E("tensor", lambda e, kc=kc, bu=bu, hh=hh, th=th, reg=reg:
                          e.matmul(bank(reg + hh * 2 + th),
                                   wqbuf[bu][:, hh * 128:hh * 128 + 128],
                                   xqT[:, kc, th * 512:th * 512 + 512],
                                   start=(kc == 0), stop=(kc == 15)),
                          wait=wmm if (hh == 0 and th == 0) else (),
                          inc=[("t", 1)] if (hh == 1 and th == 1) else ())
                slab_t[bu] = C("t")
            qt = C("t")
            for hh in range(2):
                E("scalar", lambda e, hh=hh, reg=reg: e.copy(
                    qraw[:, hh, :], bank(reg + hh * 2, 2)),
                  wait=[("t", qt)] if hh == 0 else ())
            sbar()
            outs_ = [qhat[:, hg * 2 + b_ // 2, (b_ % 2) * 512:(b_ % 2) * 512 + 512]
                     for b_ in range(4)]
            rm = rms_apply(kraw, outs_, 4, reg, qg_sb)
            rms_done[hg % 2] = rm
        k_rms_done = [rms_done[0], rms_done[1]]
        for kv in range(KVH):
            reg = (kv % 2) * 4
            for kc in range(16):
                bu = kc % 2
                wld = [("t", slab_t[bu])] if slab_t.get(bu) else []
                if kv == 0 and kc == 0:
                    wld.append(("wc", wcast_done))
                lm = DMA("sync", lambda e, kv=kv, kc=kc, bu=bu: e.dma_start(
                    out=wqbuf[bu][:, 0:128],
                    in_=wkb_h[kc * 128:kc * 128 + 128, kv * 128:kv * 128 + 128]),
                    wait=wld)
                wmm = [lm]
                if kc == 0 and k_rms_done[kv % 2]:
                    wmm.append(("v", k_rms_done[kv % 2]))
                for th in range(4):
                    E("tensor", lambda e, kc=kc, bu=bu, th=th, reg=reg: e.matmul(
                        bank(reg + th), wqbuf[bu][:, 0:128],
                        xfT[:, kc, th * 512:th * 512 + 512],
                        start=(kc == 0), stop=(kc == 15)),
                      wait=wmm if th == 0 else (),
                      inc=[("t", 1)] if th == 3 else ())
                slab_t[bu] = C("t")
            kt = C("t")
            E("scalar", lambda e, reg=reg: e.copy(qraw[:, 0, :], bank(reg, 2)),
              wait=[("t", kt)])
            E("scalar", lambda e, reg=reg: e.copy(qraw[:, 1, :], bank(reg + 2, 2)))
            sbar()
            outs_ = [khat[:, kv, b_ * 512:b_ * 512 + 512] for b_ in range(4)]
            rm = rms_apply(kraw, outs_, 4, reg, kg_sb)
            k_rms_done[kv % 2] = rm
        kv_done = C("v")
        for kc in range(16):
            DMA("sync", lambda e, kc=kc: e.dma_start(
                out=wv_sb[:, kc, :], in_=wvb_h[kc * 128:kc * 128 + 128, :]),
                wait=[("v", kv_done), ("wc", wcast_done)] if kc == 0 else ())
        wv_lms = bld.dma_totals("sync")
        v_ev = [0, 0]
        for tt in range(16):
            reg = tt % 2
            wmm = (wv_lms if tt == 0 else []) + [("v", kv_done)]
            if v_ev[reg]:
                wmm.append(("v", v_ev[reg]))
            for kc in range(16):
                E("tensor", lambda e, tt=tt, kc=kc, reg=reg: e.matmul(
                    bank(reg), xfT[:, kc, tt * 128:tt * 128 + 128], wv_sb[:, kc, :],
                    start=(kc == 0), stop=(kc == 15)),
                  wait=wmm if kc == 0 else (),
                  inc=[("t", 1)] if kc == 15 else ())
            vt = C("t")
            v_ev[reg] = E("vector", lambda e, tt=tt, reg=reg: e.tensor_copy(
                v_sb[:, tt, :], bank(reg)), wait=[("t", vt)], inc=[("v", 1)])
        v_done = C("v")

        for kc in range(16):
            DMA("sync", lambda e, kc=kc: e.dma_start(
                out=wo_sb[:, kc, :], in_=wob_h[kc * 128:kc * 128 + 128, :]),
                wait=[("v", v_done), ("wc", wcast_done)] if kc == 0 else ())
        wo_lms = bld.dma_totals("sync")

        groups = [(qb, h) for qb in range(8) for h in range(H)]
        ng = len(groups)

        def geo(qb):
            M = MPAT[qb]
            nkt = M + 1
            ngr = (nkt + 3) // 4
            return nkt, ngr, ngr * 512

        def region(qb, h):
            ngr = geo(qb)[1]
            if ngr <= 2:
                return (h % 2) * 2
            if ngr == 3:
                return (h % 2) * 3
            return 0

        bank_guard = [0, 0, 0, 0, 0, 0]
        pt_t_by_par = [0, 0]
        pt_ev = 0
        pv_ev = 0
        pvap = ps[:, 3584:3712]
        Pbuf = [P_sb, sq2]

        def emit_S(i):
            qb, h = groups[i]
            kv = h // 4
            nkt, ngr, L5 = geo(qb)
            r0 = region(qb, h)
            w = []
            gmax = max(bank_guard[r0:r0 + ngr])
            if gmax:
                w.append(("s", gmax))
            if i == 0:
                w = [("v", v_done)] + const_ms + w
            for g in range(ngr):
                E("tensor", lambda e, qb=qb, h=h, kv=kv, g=g, r0=r0: e.matmul(
                    bank(r0 + g), qhat[:, h, qb * 128:qb * 128 + 128],
                    khat[:, kv, g * 512:g * 512 + 512], start=True, stop=True),
                  wait=w if g == 0 else (),
                  inc=[("t", 1)] if g == ngr - 1 else ())
            return C("t")

        tS_ms = emit_S(0)
        for i in range(ng):
            qb, h = groups[i]
            kv = h // 4
            nkt, ngr, L5 = geo(qb)
            r0 = region(qb, h)
            Sap = ps[:, r0 * 512:r0 * 512 + L5]
            Pv = Pbuf[i % 2]
            E("vector", lambda e, qb=qb, Sap=Sap, L5=L5: e.tensor_tensor(
                Sap[:, L5 - 512:L5], Sap[:, L5 - 512:L5], masks[:, qb, :], Alu.add),
              wait=[("t", tS_ms)])
            E("vector", lambda e: e.drain())
            E("vector", lambda e, Sap=Sap: e.tensor_reduce(
                nmax[:], Sap, axis=AX, op=Alu.max, negate=True))
            E("vector", lambda e: e.drain())
            E("vector", lambda e: e.tensor_single_scalar(
                nmax[:], nmax[:], 0.08838834764831845, Alu.mult))
            vmx = vbar()
            wexp = [("v", vmx)]
            if pt_t_by_par[i % 2]:
                wexp.append(("t", pt_t_by_par[i % 2]))
            E("scalar", lambda e, Sap=Sap, L5=L5, Pv=Pv: e.activation(
                Pv[:, 0:L5], Sap, Act.Exp, bias=nmax[:],
                scale=0.08838834764831845, accum_out=rsum[:]), wait=wexp)
            em = sbar()
            for b_ in range(r0, r0 + ngr):
                bank_guard[b_] = em
            E("scalar", lambda e: e.activation(rcp[:], rsum[:], Act.Ln))
            E("scalar", lambda e: e.drain())
            E("scalar", lambda e: e.activation(rcp[:], rcp[:], Act.Exp, scale=-1.0))
            E("scalar", lambda e: e.drain())
            E("scalar", lambda e, L5=L5, Pv=Pv: e.activation(
                Pv[:, 0:L5], Pv[:, 0:L5], Act.Identity, scale=rcp[:]))
            vsc = ("s_", 0)
            vsc = sbar()
            if i + 1 < ng:
                tS_ms = emit_S(i + 1)
            wPT = [("s", vsc)]
            nw = (nkt + 7) // 8
            for w_ in range(nw):
                t0 = w_ * 8
                t1 = min(nkt, t0 + 8)
                for kt in range(t0, t1):
                    E("tensor", lambda e, kt=kt, t0=t0, Pv=Pv: e.transpose(
                        bank_bf(6)[:, (kt - t0) * 128:(kt - t0) * 128 + 128],
                        Pv[:, kt * 128:kt * 128 + 128], id_sb[:]),
                      wait=wPT if kt == t0 else (),
                      inc=[("t", 1)] if kt == t1 - 1 else ())
                tw = C("t")
                if w_ == nw - 1:
                    pt_t_by_par[i % 2] = tw
                pt_ev = E("scalar", lambda e, t0=t0, t1=t1: e.copy(
                    PT_sb[:, t0:t1, :],
                    bank_bf(6)[:, 0:(t1 - t0) * 128].rearrange(
                        "p (a b) -> p a b", b=128)),
                    wait=[("t", tw)], inc=[("s", 1)])
                wPT = [("s", pt_ev)]
            wPV = [("s", pt_ev)]
            if pv_ev:
                wPV.append(("s", pv_ev))
            for kt in range(nkt):
                E("tensor", lambda e, kt=kt, kv=kv: e.matmul(
                    pvap, v_sb[:, kt, kv * 128:kv * 128 + 128], PT_sb[:, kt, :],
                    start=(kt == 0), stop=(kt == nkt - 1)),
                  wait=wPV if kt == 0 else (),
                  inc=[("t", 1)] if kt == nkt - 1 else ())
            tpv = C("t")
            pv_ev = E("scalar", lambda e, h=h, qb=qb: e.copy(
                ao_sb[:, h, qb * 128:qb * 128 + 128], pvap),
                wait=[("t", tpv)], inc=[("s", 1)])
        a3_done_s = C("s")

        ln1_v_prev = 0
        y1t_last_t = 0
        y1t_ev = [0, 0]
        xres_used = [0, 0]
        for tc in range(8):
            bu = tc % 2
            wxr = [("s", a3_done_s)] if tc < 2 else [("v", xres_used[bu])]
            xm = DMA("sync", lambda e, tc=tc, bu=bu: e.dma_start(
                out=xres[bu][:], in_=x_q[tc * 128:tc * 128 + 128, :]),
                wait=wxr)
            wmm = [("s", a3_done_s)] + (wo_lms if tc == 0 else [])
            if ln1_v_prev:
                wmm.append(("v", ln1_v_prev))
            for og in range(4):
                for kc in range(16):
                    E("tensor", lambda e, tc=tc, og=og, kc=kc: e.matmul(
                        bank(og), ao_sb[:, kc, tc * 128:tc * 128 + 128],
                        wo_sb[:, kc, og * 512:og * 512 + 512],
                        start=(kc == 0), stop=(kc == 15)),
                      wait=wmm if (og == 0 and kc == 0) else (),
                      inc=[("t", 1)] if (og == 3 and kc == 15) else ())
            tm = C("t")
            for og in range(4):
                E("vector", lambda e, og=og, bu=bu: e.scalar_tensor_tensor(
                    y1pre[:, og * 512:og * 512 + 512],
                    xres[bu][:, og * 512:og * 512 + 512], 2.0, bank(og),
                    op0=Alu.mult, op1=Alu.add),
                  wait=[("t", tm), xm] if og == 0 else ())
            E("vector", lambda e: e.drain())
            E("vector", lambda e: e.tensor_reduce(nmean[:], y1pre[:], axis=AX,
                                                  op=Alu.add, negate=True))
            E("vector", lambda e: e.drain())
            E("vector", lambda e: e.tensor_single_scalar(
                nmean[:], nmean[:], 1.0 / 2048, Alu.mult))
            vst = vbar()
            xres_used[bu] = vst
            E("scalar", lambda e: e.activation(sq2[:, 0:2048], y1pre[:], Act.Square,
                                               bias=nmean[:], accum_out=ssd[:]),
              wait=[("v", vst)])
            E("scalar", lambda e: e.drain())
            E("scalar", lambda e: e.activation(rstd[:], ssd[:], Act.Ln,
                                               scale=1.0 / 2048, bias=eps5[:]))
            E("scalar", lambda e: e.drain())
            E("scalar", lambda e: e.activation(rstd[:], rstd[:], Act.Exp, scale=-0.5))
            sst = sbar()
            wln = [("s", sst)]
            if y1t_last_t:
                wln.append(("t", y1t_last_t))
            E("vector", lambda e: e.tensor_scalar(
                P_sb[:], y1pre[:], nmean[:], rstd[:], op0=Alu.add, op1=Alu.mult),
              wait=wln)
            ln1_v_prev = vbar()
            for bh in range(2):
                wt2 = [("v", ln1_v_prev)]
                if y1t_ev[bh]:
                    wt2.append(("v", y1t_ev[bh]))
                for kcl in range(8):
                    kc = bh * 8 + kcl
                    E("tensor", lambda e, bh=bh, kcl=kcl, kc=kc: e.transpose(
                        bank_bf(4 + bh)[0:64, kcl * 128:kcl * 128 + 128],
                        P_sb[:, kc * 128:kc * 128 + 128:2], id_sb[:]),
                      wait=wt2 if kcl == 0 else ())
                    E("tensor", lambda e, bh=bh, kcl=kcl, kc=kc: e.transpose(
                        bank_bf(4 + bh)[64:128, kcl * 128:kcl * 128 + 128],
                        P_sb[:, kc * 128 + 1:kc * 128 + 128:2], id_sb[:]),
                      inc=[("t", 1)] if kcl == 7 else ())
                tm2 = C("t")
                y1t_ev[bh] = E("vector", lambda e, bh=bh, tc=tc: e.tensor_copy(
                    y1T[:, bh * 8:bh * 8 + 8, tc * 128:tc * 128 + 128],
                    bank_bf(4 + bh)[:].rearrange("p (a b) -> p a b", b=128)),
                    wait=[("t", tm2)], inc=[("v", 1)])
            y1t_last_t = C("t")
        a4_done_v = C("v")

        gu_mm = [0, 0]
        inter_ev = [0, 0]
        dn_mm = [0, 0]
        y2_add = [0, 0]
        dn_last_t = 0
        first = True
        for sc in range(8):
            for nc2 in range(16):
                nch = sc * 16 + nc2
                par = nc2 % 2
                wld = ([("t", gu_mm[par])] if gu_mm[par]
                       else [("cc", ag_done), ("v", a4_done_v)])
                glm1 = DMA("sync", lambda e, nch=nch, par=par: e.dma_start(
                    out=gub[:, par * 2, :, :], in_=g_fu.ap()[nch]), wait=wld)
                glm2 = DMA("sync", lambda e, nch=nch, par=par: e.dma_start(
                    out=gub[:, par * 2 + 1, :, :], in_=u_fu.ap()[nch]))
                reg = par * 4
                wmm = [glm1, glm2]
                if inter_ev[par]:
                    wmm.append(("v", inter_ev[par]))
                if sc > 0 and nc2 < 2:
                    wmm += [("v", y2_add[0]), ("v", y2_add[1])]
                if sc == 0 and nc2 < 2:
                    wmm.append(("v", a4_done_v))
                for kc in range(16):
                    for th in range(2):
                        E("tensor", lambda e, par=par, kc=kc, th=th, reg=reg:
                          e.matmul(bank(reg + th), gub[:, par * 2, kc, :],
                                   y1T[:, kc, th * 512:th * 512 + 512],
                                   start=(kc == 0), stop=(kc == 15)),
                          wait=wmm if (kc == 0 and th == 0) else ())
                        E("tensor", lambda e, par=par, kc=kc, th=th, reg=reg:
                          e.matmul(bank(reg + 2 + th), gub[:, par * 2 + 1, kc, :],
                                   y1T[:, kc, th * 512:th * 512 + 512],
                                   start=(kc == 0), stop=(kc == 15)),
                          inc=[("t", 1)] if (kc == 15 and th == 1) else ())
                gu_mm[par] = C("t")
                wge = [("t", gu_mm[par])]
                if inter_ev[par]:
                    wge.append(("v", inter_ev[par]))
                E("scalar", lambda e, par=par, reg=reg: e.activation(
                    sq2[:, par * 1024:par * 1024 + 1024], bank(reg, 2),
                    Act.Gelu_apprx_tanh), wait=wge)
                sg = sbar()
                wiv = [("s", sg)]
                if dn_last_t:
                    wiv.append(("t", dn_last_t))
                inter_ev[par] = E("vector", lambda e, nc2=nc2, par=par, reg=reg:
                                  e.tensor_tensor(
                                      inter[:, nc2, :], bank(reg + 2, 2),
                                      sq2[:, par * 1024:par * 1024 + 1024], Alu.mult),
                                  wait=wiv, inc=[("v", 1)])
            iv = C("v")
            for hc in range(16):
                par = hc % 2
                wld = [("t", dn_mm[par])] if dn_mm[par] else [("cc", ag_done),
                                                             ("v", a4_done_v)]
                dlm = DMA("sync", lambda e, sc=sc, hc=hc, par=par: e.dma_start(
                    out=dnb[:, par, :, :], in_=d_fu.ap()[sc * 16 + hc]),
                    wait=wld)
                reg = par * 2
                wmm = [dlm, ("v", iv)]
                if y2_add[par]:
                    wmm.append(("v", y2_add[par]))
                for nc2 in range(16):
                    for th in range(2):
                        E("tensor", lambda e, par=par, nc2=nc2, th=th, reg=reg:
                          e.matmul(bank(reg + th), dnb[:, par, nc2, :],
                                   inter[:, nc2, th * 512:th * 512 + 512],
                                   start=(nc2 == 0), stop=(nc2 == 15)),
                          wait=wmm if (nc2 == 0 and th == 0) else (),
                          inc=[("t", 1)] if (nc2 == 15 and th == 1) else ())
                dn_mm[par] = C("t")
                if first and hc < 2:
                    y2_add[par] = E("vector", lambda e, hc=hc, reg=reg: e.tensor_copy(
                        y2acc[:, hc, :], bank(reg, 2)),
                        wait=[("t", dn_mm[par])], inc=[("v", 1)])
                elif first:
                    y2_add[par] = E("vector", lambda e, hc=hc, reg=reg: e.tensor_copy(
                        y2acc[:, hc, :], bank(reg, 2)),
                        wait=[("t", dn_mm[par]), ("v", y2_add[par])], inc=[("v", 1)])
                else:
                    y2_add[par] = E("vector", lambda e, hc=hc, reg=reg:
                                    e.scalar_tensor_tensor(
                                        y2acc[:, hc, :], bank(reg, 2), 1.0,
                                        y2acc[:, hc, :], op0=Alu.mult, op1=Alu.add),
                                    wait=[("t", dn_mm[par]), ("v", y2_add[par])],
                                    inc=[("v", 1)])
            dn_last_t = C("t")
            first = False
        ffn_done_v = C("v")

        for kc in range(16):
            E("vector", lambda e, kc=kc: e.tensor_tensor(
                y2acc[:, kc, :], y2acc[:, kc, :], y1T[:, kc, :], Alu.add),
              wait=[("v", ffn_done_v)] if kc == 0 else ())
        E("vector", lambda e: e.drain())
        for kc in range(16):
            E("vector", lambda e, kc=kc: e.tensor_copy(y1T[:, kc, :], y2acc[:, kc, :]))
        vy2b = vbar()
        E("scalar", lambda e: e.activation(ar2[:], ar3[:], Act.Square),
          wait=[("v", vy2b)])
        ssq = sbar()
        st_ms = [0, 0]
        vvar_prev = 0
        for tb in range(2):
            w0 = [("s", ssq)] if tb == 0 else [("v", vvar_prev)]
            for kc in range(16):
                E("tensor", lambda e, tb=tb, kc=kc: e.matmul(
                    bank(0)[0:1, 0:512], ones_b[:],
                    y1T[:, kc, tb * 512:tb * 512 + 512],
                    start=(kc == 0), stop=(kc == 15)),
                  wait=w0 if kc == 0 else ())
            for kc in range(16):
                E("tensor", lambda e, tb=tb, kc=kc: e.matmul(
                    bank(1)[0:1, 0:512], ones_b[:],
                    inter[:, kc, tb * 512:tb * 512 + 512],
                    start=(kc == 0), stop=(kc == 15)),
                  inc=[("t", 1)] if kc == 15 else ())
            tsum = C("t")
            E("vector", lambda e: e.tensor_single_scalar(
                ln2a[:], bank(0)[0:1, 0:512], -1.0 / 2048, Alu.mult),
              wait=[("t", tsum)])
            E("vector", lambda e: e.tensor_single_scalar(
                ln2b[:], bank(1)[0:1, 0:512], 1.0 / 2048, Alu.mult))
            E("vector", lambda e: e.drain())
            E("vector", lambda e: e.tensor_tensor(ln2c[:], ln2a[:], ln2a[:], Alu.mult))
            E("vector", lambda e: e.drain())
            E("vector", lambda e: e.tensor_tensor(ln2b[:], ln2b[:], ln2c[:],
                                                  Alu.subtract))
            vvar_prev = vvar = vbar()
            E("scalar", lambda e: e.activation(ln2c[:], ln2b[:], Act.Ln,
                                               bias=eps5[0:1, :]),
              wait=[("v", vvar)])
            E("scalar", lambda e: e.drain())
            E("scalar", lambda e: e.activation(ln2c[:], ln2c[:], Act.Exp, scale=-0.5))
            srs = sbar()
            wbc = [("s", srs)]
            if st_ms[0]:
                wbc.append(("v", st_ms[0]))
            E("tensor", lambda e: e.matmul(bank(2), ones_f[:], ln2a[:],
                                           start=True, stop=True), wait=wbc)
            E("tensor", lambda e: e.matmul(bank(3), ones_f[:], ln2c[:],
                                           start=True, stop=True), inc=[("t", 1)])
            tbc = C("t")
            for kc in range(16):
                E("vector", lambda e, tb=tb, kc=kc: e.tensor_tensor(
                    y2acc[:, kc, tb * 512:tb * 512 + 512],
                    y2acc[:, kc, tb * 512:tb * 512 + 512], bank(2), Alu.add),
                  wait=[("t", tbc)] if kc == 0 else ())
            E("vector", lambda e: e.drain())
            for kc in range(16):
                E("vector", lambda e, tb=tb, kc=kc: e.tensor_tensor(
                    y2acc[:, kc, tb * 512:tb * 512 + 512],
                    y2acc[:, kc, tb * 512:tb * 512 + 512], bank(3), Alu.mult))
            st_ms[tb] = vbar()
        ev_fin = 0
        st_fin = [0, 0]
        for tc in range(8):
            bu = tc % 2
            wt = [("v", st_ms[tc // 4])]
            if ev_fin:
                wt.append(("v", ev_fin))
            for kc in range(16):
                bk = 4 + kc // 4
                E("tensor", lambda e, tc=tc, kc=kc, bk=bk: e.transpose(
                    bank(bk)[:, (kc % 4) * 128:(kc % 4) * 128 + 128],
                    y2acc[:, kc, tc * 128:tc * 128 + 128], idf_sb[:]),
                  wait=wt if kc == 0 else (),
                  inc=[("t", 1)] if kc == 15 else ())
            tfin = C("t")
            wev = [("t", tfin)]
            if st_fin[bu]:
                wev.append(st_fin[bu])
            for bk in range(4):
                E("vector", lambda e, bu=bu, bk=bk: e.tensor_copy(
                    y2tok[bu][:, bk * 512:bk * 512 + 512].rearrange(
                        "p (a j two) -> p a two j", a=4, two=2),
                    bank(4 + bk)[:].rearrange("p (a b) -> p a b", b=128).rearrange(
                        "p a (two j) -> p a two j", two=2)),
                  wait=wev if bk == 0 else ())
            ev_fin = vbar()
            st_fin[bu] = DMA("gpsimd", lambda e, tc=tc, bu=bu: e.dma_start(
                out=y_h[tc * 128:tc * 128 + 128, :], in_=y2tok[bu][:]),
                wait=[("v", ev_fin)])
        E("gpsimd", lambda e: e.nop(),
          wait=bld.dma_totals("gpsimd") + bld.dma_totals("sync"))

        for ename in ENGS:
            entries = bld.entries[ename]

            def runner(eng, _entries=entries):
                for wait, f, incs in _entries:
                    for s_, v_ in wait:
                        eng.wait_ge(SEM[s_], v_)
                    ins = f(eng)
                    for s_, n_ in incs:
                        ins.then_inc(SEM[s_], n_)
            getattr(blk, ename)(runner)

    n_inst = sum(len(v) for v in bld.entries.values())
    print(f"[kernel] emitted {n_inst} entries "
          + str({k: len(v) for k, v in bld.entries.items()}))
    return nc


def kernel(**inputs):
    x = np.asarray(inputs["L0_attn_x"], dtype=np.float32)
    wq = np.asarray(inputs["L0_attn_Wq"], dtype=np.float32)
    wk = np.asarray(inputs["L0_attn_Wk"], dtype=np.float32)
    wv = np.asarray(inputs["L0_attn_Wv"], dtype=np.float32)
    wo = np.asarray(inputs["L0_attn_Wo"], dtype=np.float32)
    qg = np.ascontiguousarray(
        np.asarray(inputs["L0_attn_q_gamma"], dtype=np.float32).reshape(128, 1))
    kg = np.ascontiguousarray(
        np.asarray(inputs["L0_attn_k_gamma"], dtype=np.float32).reshape(128, 1))
    gp = np.asarray(inputs["L0_ffn_gate_W_packed"], dtype=np.int32)
    gs = np.asarray(inputs["L0_ffn_gate_scales"], dtype=np.float32)
    up = np.asarray(inputs["L0_ffn_up_W_packed"], dtype=np.int32)
    us = np.asarray(inputs["L0_ffn_up_scales"], dtype=np.float32)
    dp = np.asarray(inputs["L0_ffn_down_W_packed"], dtype=np.int32)
    ds = np.asarray(inputs["L0_ffn_down_scales"], dtype=np.float32)

    ident = np.eye(128, dtype=ml_dtypes.bfloat16)
    identf = np.eye(128, dtype=np.float32)

    in_maps = []
    for c in range(NCORES):
        b, par = c // 2, c % 2
        blocks = QBLOCKS[par]
        xq_rows = np.concatenate([x[b, g * 128:(g + 1) * 128, :] for g in blocks], 0)
        mk = np.zeros((8, 128, 512), dtype=np.float32)
        for qb, g in enumerate(blocks):
            L5 = ((MPAT[qb] + 1 + 3) // 4) * 512
            k0 = L5 - 512
            kk = k0 + np.arange(512)[None, :]
            qq = g * 128 + np.arange(128)[:, None]
            mk[qb] = np.where(kk <= qq, 0.0, NEG)
        in_maps.append({
            "x_full": np.ascontiguousarray(x[b]),
            "x_q": np.ascontiguousarray(xq_rows),
            "wq": wq, "wk": wk, "wv": wv, "wo": wo,
            "gp": np.ascontiguousarray(gp[c * 2048:(c + 1) * 2048]),
            "gs": np.ascontiguousarray(gs[c * 2048:(c + 1) * 2048]),
            "up": np.ascontiguousarray(up[c * 2048:(c + 1) * 2048]),
            "us": np.ascontiguousarray(us[c * 2048:(c + 1) * 2048]),
            "dp": np.ascontiguousarray(dp[:, c * 1024:(c + 1) * 1024]),
            "ds": np.ascontiguousarray(ds[:, c * 64:(c + 1) * 64]),
            "qg": qg, "kg": kg, "ident": ident, "identf": identf, "maskqb": mk,
        })

    nc = build_nc()
    res = run_bass_kernel_spmd(nc, in_maps, core_ids=list(range(NCORES)))

    out = np.empty((B, S, D), dtype=np.float32)
    for c in range(NCORES):
        b, par = c // 2, c % 2
        y = res.results[c]["y"]
        for qb, g in enumerate(QBLOCKS[par]):
            out[b, g * 128:(g + 1) * 128, :] = y[qb * 128:(qb + 1) * 128, :]
    return out
